# version 01 (final)
# baseline (speedup 1.0000x reference)
import numpy as np
from contextlib import ExitStack, contextmanager

import concourse.bass as bass
import concourse.mybir as mybir
from concourse.bass_utils import run_bass_kernel_spmd

F32 = mybir.dt.float32
BF16 = mybir.dt.bfloat16
AF = mybir.ActivationFunctionType
ALU = mybir.AluOpType
AX = mybir.AxisListType.X

NCORES = 8
D = 1024
N = 16384
TS = 2048
CW = 512
NCH = TS // CW
T = 4096
B = 4

GELU_C0 = float(np.sqrt(2.0 / np.pi))
GELU_C1 = 0.044715


class Prog:
    def __init__(self, name):
        self.nc = bass.Bass("TRN2", target_bir_lowering=False, debug=False)
        self.es = ExitStack()
        self.es.__enter__()
        self.ld = self.es.enter_context(self.nc.semaphore("ld"))
        self.st = self.es.enter_context(self.nc.semaphore("st"))
        self.ldc = 0
        self.stc = 0
        self._n = 0
        self.name = name

    def inp(self, name, shape, dt=F32):
        return self.nc.dram_tensor(name, list(shape), dt, kind="ExternalInput").ap()

    def out(self, name, shape, dt=F32):
        return self.nc.dram_tensor(name, list(shape), dt, kind="ExternalOutput").ap()

    def sb(self, shape, dt=F32):
        self._n += 1
        return self.es.enter_context(self.nc.sbuf_tensor(f"s{self._n}", list(shape), dt))

    def ps(self, shape, dt=F32):
        self._n += 1
        return self.es.enter_context(self.nc.psum_tensor(f"p{self._n}", list(shape), dt))

    def load(self, pairs):
        ldc0 = self.ldc
        npairs = len(pairs)
        with self.nc.Block() as b:
            @b.sync
            def _(s, pairs=pairs, base=ldc0, n=npairs):
                for o, i in pairs:
                    s.dma_start(out=o, in_=i).then_inc(self.ld, 16)
                s.wait_ge(self.ld, base + 16 * n)
        self.ldc += 16 * npairs
        self.nc.all_engine_barrier()

    def store(self, pairs):
        stc0 = self.stc
        npairs = len(pairs)
        with self.nc.Block() as b:
            @b.sync
            def _(s, pairs=pairs, base=stc0, n=npairs):
                for o, i in pairs:
                    s.dma_start(out=o, in_=i).then_inc(self.st, 16)
                s.wait_ge(self.st, base + 16 * n)
        self.stc += 16 * npairs
        self.nc.all_engine_barrier()

    @contextmanager
    def stage(self, eng):
        holder = {}
        with self.nc.Block() as b:
            def run(fn):
                getattr(b, eng)(fn)
            holder["run"] = run
            yield holder
        self.nc.all_engine_barrier()

    def eng(self, engname, fn):
        with self.nc.Block() as b:
            getattr(b, engname)(fn)
        self.nc.all_engine_barrier()

    def tensor(self, fn):
        self.eng("tensor", fn)

    def vector(self, fn):
        self.eng("vector", fn)

    def scalar(self, fn):
        self.eng("scalar", fn)

    def finish(self):
        with self.nc.Block() as b:
            @b.sync
            def _(s):
                s.wait_ge(self.st, self.stc)
        self.es.close()
        return self.nc



def emit_weight_load(p, wdram, k0, nk, M, wst_f32, wbf):
    for j in range(nk):
        p.load([(wst_f32[:, 0:M], wdram[(k0 + j) * 128:(k0 + j + 1) * 128, 0:M])])
        def cast(s, j=j, M=M):
            s.copy(out=wbf[:, j * M:(j + 1) * M], in_=wst_f32[:, 0:M])
        p.scalar(cast)


def emit_matmul(p, wbf, M, nk, act_bf_tiles, psbanks, m_list, n, start=True):
    def mm(te):
        for mi, m in enumerate(m_list):
            for k in range(nk):
                te.matmul(psbanks[mi][:, 0:n],
                          wbf[:, k * M + m * 128: k * M + m * 128 + 128],
                          act_bf_tiles[k][:, 0:n],
                          start=(start and k == 0), stop=(k == nk - 1))
    p.tensor(mm)


def emit_ln(p, tiles, n, ones_col, ps_s1, ps_s2, ps_b1, ps_b2, sq, row1, row2,
           gam_cols=None, bet_cols=None, out_tiles=None, nf=8, dmodel=1024.0):
    if out_tiles is None:
        out_tiles = tiles
    def squares(s):
        for f in range(nf):
            s.activation(out=sq[f][:, 0:n], in_=tiles[f][:, 0:n], func=AF.Square)
    p.scalar(squares)

    def stats(te):
        for f in range(nf):
            te.matmul(ps_s1[:, 0:n], ones_col[:], tiles[f][:, 0:n],
                      start=(f == 0), stop=(f == nf - 1))
        for f in range(nf):
            te.matmul(ps_s2[:, 0:n], ones_col[:], sq[f][:, 0:n],
                      start=(f == 0), stop=(f == nf - 1))
    p.tensor(stats)

    inv = 1.0 / dmodel
    def rowops(v):
        v.tensor_scalar_mul(row1[:, 0:n], ps_s1[0:1, 0:n], inv)
        v.tensor_scalar_mul(row2[:, 0:n], ps_s2[0:1, 0:n], inv)
        v.scalar_tensor_tensor(row2[:, 0:n], row1[:, 0:n], 1.0, row2[:, 0:n],
                               ALU.mult, ALU.bypass)
    def rowops2(v):
        v.tensor_mul(sq[0][0:1, 0:n], row1[:, 0:n], row1[:, 0:n])
        v.tensor_sub(row2[:, 0:n], row2[:, 0:n], sq[0][0:1, 0:n])
        v.tensor_scalar_add(row2[:, 0:n], row2[:, 0:n], 1e-5)
    p.vector(lambda v: (rowops(v), rowops2(v)))

    def rstd(s):
        s.activation(out=row2[:, 0:n], in_=row2[:, 0:n], func=AF.Ln)
        s.activation(out=row2[:, 0:n], in_=row2[:, 0:n], func=AF.Exp, scale=-0.5)
    p.scalar(rstd)

    def bcast(te):
        te.matmul(ps_b1[:, 0:n], ones_col[0:1, 0:1], row1[:, 0:n], start=True, stop=True)
        te.matmul(ps_b2[:, 0:n], ones_col[0:1, 0:1], row2[:, 0:n], start=True, stop=True)
    p._bcast = bcast

    return out_tiles



_PROG_CACHE = {}
DBG = {}


def bld_k1():
    p = Prog("k1")
    xT = p.inp("xT", (D, TS))
    W_in = p.inp("W_in", (D, 4096))
    b_in = p.inp("b_in", (4096, 1))
    gamma = p.inp("gamma", (D, 1))
    beta = p.inp("beta", (D, 1))
    projT = p.out("projT", (4096, TS))

    nf = 8
    x_t = [p.sb((128, CW)) for _ in range(nf)]
    sq = [p.sb((128, CW)) for _ in range(nf)]
    xb = [p.sb((128, CW), BF16) for _ in range(nf)]
    wst = p.sb((128, 4096))
    wbf = p.sb((128, 8 * 4096), BF16)
    gcol = [p.sb((128, 1)) for _ in range(nf)]
    bcol = [p.sb((128, 1)) for _ in range(nf)]
    bincol = [p.sb((128, 1)) for _ in range(32)]
    ones_col = p.sb((128, 1))
    ones_row = p.sb((1, 128))
    row1 = p.sb((1, CW))
    row2 = p.sb((1, CW))
    otile = p.sb((128, CW))
    ps_g = [p.ps((128, CW)) for _ in range(4)]
    ps_s1 = p.ps((1, CW))
    ps_s2 = p.ps((1, CW))
    ps_b1 = p.ps((128, CW))
    ps_b2 = p.ps((128, CW))

    p.vector(lambda v: (v.memset(ones_col[:], 1.0), v.memset(ones_row[:], 1.0)))
    p.load([(gcol[f][:], gamma[f * 128:(f + 1) * 128, :]) for f in range(nf)]
           + [(bcol[f][:], beta[f * 128:(f + 1) * 128, :]) for f in range(nf)]
           + [(bincol[m][:], b_in[m * 128:(m + 1) * 128, :]) for m in range(32)])
    emit_weight_load(p, W_in, 0, 8, 4096, wst, wbf)

    for ci in range(NCH):
        c0 = ci * CW
        p.load([(x_t[f][:], xT[f * 128:(f + 1) * 128, c0:c0 + CW]) for f in range(nf)])
        p.scalar(lambda s: [s.activation(out=sq[f][:], in_=x_t[f][:], func=AF.Square)
                            for f in range(nf)])
        def stats(te):
            for f in range(nf):
                te.matmul(ps_s1[:], ones_col[:], x_t[f][:], start=(f == 0), stop=(f == nf - 1))
            for f in range(nf):
                te.matmul(ps_s2[:], ones_col[:], sq[f][:], start=(f == 0), stop=(f == nf - 1))
        p.tensor(stats)
        def rowops(v):
            v.tensor_scalar_mul(row1[:], ps_s1[0:1, :], 1.0 / 1024.0)
            v.tensor_scalar_mul(row2[:], ps_s2[0:1, :], 1.0 / 1024.0)
            v.tensor_mul(sq[0][0:1, :], row1[:], row1[:])
            v.tensor_sub(row2[:], row2[:], sq[0][0:1, :])
            v.tensor_scalar_add(row2[:], row2[:], 1e-5)
        p.vector(rowops)
        p.scalar(lambda s: (s.activation(out=row2[:], in_=row2[:], func=AF.Ln),
                            s.activation(out=row2[:], in_=row2[:], func=AF.Exp, scale=-0.5)))
        def bc(te):
            te.matmul(ps_b1[:], ones_row[:], row1[:], start=True, stop=True)
            te.matmul(ps_b2[:], ones_row[:], row2[:], start=True, stop=True)
        p.tensor(bc)
        def norm(v):
            for f in range(nf):
                v.tensor_sub(sq[f][:], x_t[f][:], ps_b1[:])
                v.tensor_mul(sq[f][:], sq[f][:], ps_b2[:])
                v.tensor_scalar(sq[f][:], sq[f][:], gcol[f][:], bcol[f][:], ALU.mult, ALU.add)
        p.vector(norm)
        p.scalar(lambda s: [s.copy(out=xb[f][:], in_=sq[f][:]) for f in range(nf)])
        for mg in range(8):
            mlist = [mg * 4 + i for i in range(4)]
            emit_matmul(p, wbf, 4096, 8, xb, ps_g, mlist, CW)
            def evac(v, mlist=mlist, c0=c0):
                for mi, m in enumerate(mlist):
                    v.tensor_scalar_add(otile[:], ps_g[mi][:], bincol[m][:])
                    v.tensor_copy(otile[:], otile[:])
            for mi, m in enumerate(mlist):
                p.vector(lambda v, mi=mi, m=m: v.tensor_scalar_add(otile[:], ps_g[mi][:], bincol[m][:]))
                p.store([(projT[m * 128:(m + 1) * 128, c0:c0 + CW], otile[:])])
    return p.finish()


def bld_k2():
    p = Prog("k2")
    projS = p.inp("projS", (4096, TS + 2))
    conv_w = p.inp("conv_w", (3 * 2048, 1))
    conv_b = p.inp("conv_b", (2048, 1))
    Wa = p.inp("Wa", (2048, 2048))
    Wb = p.inp("Wb", (2048, 2048))
    Wc = p.inp("Wc", (2048, 2048))
    ba = p.inp("ba", (2048, 1))
    bb = p.inp("bb", (2048, 1))
    bc_ = p.inp("bc", (2048, 1))
    aT = p.out("aT", (2048, TS))
    vT = p.out("vT", (2048, TS))
    cT = p.out("cT", (2048, TS))
    uT = p.out("uT", (2048, TS))

    nf = 16
    pt = p.sb((128, CW + 2))
    gt = p.sb((128, CW))
    c1 = p.sb((128, CW))
    c2 = p.sb((128, CW))
    cwt = [p.sb((128, 1)) for _ in range(3 * nf)]
    cbt = [p.sb((128, 1)) for _ in range(nf)]
    bat = [p.sb((128, 1)) for _ in range(nf)]
    bbt = [p.sb((128, 1)) for _ in range(nf)]
    bct = [p.sb((128, 1)) for _ in range(nf)]
    ubf = [[p.sb((128, CW), BF16) for _ in range(nf)] for _ in range(NCH)]
    uf32 = [p.sb((128, CW)) for _ in range(nf)]
    wst = p.sb((128, 2048))
    wbf = p.sb((128, 16 * 2048), BF16)
    ps_g = [p.ps((128, CW)) for _ in range(4)]
    ot = p.sb((128, CW))
    ot2 = p.sb((128, CW))

    p.load([(cwt[3 * f + j][:], conv_w[j * 2048 + f * 128:j * 2048 + (f + 1) * 128, :])
            for f in range(nf) for j in range(3)]
           + [(cbt[f][:], conv_b[f * 128:(f + 1) * 128, :]) for f in range(nf)]
           + [(bat[f][:], ba[f * 128:(f + 1) * 128, :]) for f in range(nf)]
           + [(bbt[f][:], bb[f * 128:(f + 1) * 128, :]) for f in range(nf)]
           + [(bct[f][:], bc_[f * 128:(f + 1) * 128, :]) for f in range(nf)])

    for ci in range(NCH):
        c0 = ci * CW
        for f in range(nf):
            p.load([(pt[:], projS[f * 128:(f + 1) * 128, c0:c0 + CW + 2]),
                    (gt[:], projS[(16 + f) * 128:(17 + f) * 128, c0 + 2:c0 + CW + 2])])
            def conv(v, f=f):
                v.tensor_scalar(c1[:], pt[:, 2:CW + 2], cwt[3 * f + 2][:], cbt[f][:], ALU.mult, ALU.add)
                v.scalar_tensor_tensor(c2[:], pt[:, 1:CW + 1], cwt[3 * f + 1][:], c1[:], ALU.mult, ALU.add)
                v.scalar_tensor_tensor(c1[:], pt[:, 0:CW], cwt[3 * f + 0][:], c2[:], ALU.mult, ALU.add)
            p.vector(conv)
            p.scalar(lambda s: (s.activation(out=c1[:], in_=c1[:], func=AF.Silu),
                                s.activation(out=c2[:], in_=gt[:], func=AF.Sigmoid)))
            p.vector(lambda v: v.tensor_mul(c1[:], c1[:], c2[:]))
            p.store([(uT[f * 128:(f + 1) * 128, c0:c0 + CW], c1[:])])
            p.scalar(lambda s, ci=ci, f=f: s.copy(out=ubf[ci][f][:], in_=c1[:]))

    for wi, (W, bias, outT, act) in enumerate([(Wa, bat, aT, "sig"),
                                               (Wb, bbt, vT, "sigmul"),
                                               (Wc, bct, cT, "tanh")]):
        emit_weight_load(p, W, 0, 16, 2048, wst, wbf)
        for ci in range(NCH):
            c0 = ci * CW
            if act == "sigmul":
                p.load([(uf32[f][:], uT[f * 128:(f + 1) * 128, c0:c0 + CW])
                        for f in range(nf)])
            for mg in range(4):
                mlist = [mg * 4 + i for i in range(4)]
                emit_matmul(p, wbf, 2048, 16, ubf[ci], ps_g, mlist, CW)
                for mi, m in enumerate(mlist):
                    if act == "sig":
                        p.scalar(lambda s, mi=mi, m=m: s.activation(
                            out=ot[:], in_=ps_g[mi][:], func=AF.Sigmoid, bias=bias[m][:]))
                    elif act == "tanh":
                        p.scalar(lambda s, mi=mi, m=m: s.activation(
                            out=ot[:], in_=ps_g[mi][:], func=AF.Tanh, bias=bias[m][:]))
                    else:
                        p.scalar(lambda s, mi=mi, m=m: s.activation(
                            out=ot2[:], in_=ps_g[mi][:], func=AF.Sigmoid, bias=bias[m][:]))
                        p.vector(lambda v, m=m: v.tensor_mul(ot[:], ot2[:], uf32[m][:]))
                    p.store([(outT[m * 128:(m + 1) * 128, c0:c0 + CW], ot[:])])
    return p.finish()


def bld_k3():
    p = Prog("k3")
    R = 256
    aR = p.inp("aR", (R, N))
    vR = p.inp("vR", (R, N))
    cR = p.inp("cR", (R, N))
    uR = p.inp("uR", (R, N))
    yR = p.out("yR", (R, N))
    at = p.sb((128, T)); vt = p.sb((128, T)); ct = p.sb((128, T)); ut = p.sb((128, T))
    ht = p.sb((128, T))
    for rt in range(R // 128):
        for b in range(B):
            r0, cc0 = rt * 128, b * T
            p.load([(at[:], aR[r0:r0 + 128, cc0:cc0 + T]),
                    (vt[:], vR[r0:r0 + 128, cc0:cc0 + T]),
                    (ct[:], cR[r0:r0 + 128, cc0:cc0 + T]),
                    (ut[:], uR[r0:r0 + 128, cc0:cc0 + T])])
            p.vector(lambda v: v.tensor_tensor_scan(ht[:], at[:], vt[:], 0.0, ALU.mult, ALU.add))
            p.vector(lambda v: (v.tensor_mul(ht[:], ht[:], ct[:]),
                                v.tensor_add(ht[:], ht[:], ut[:])))
            p.store([(yR[r0:r0 + 128, cc0:cc0 + T], ht[:])])
    return p.finish()


def bld_k4():
    p = Prog("k4")
    yT = p.inp("yT", (2048, TS))
    xT = p.inp("xT", (D, TS))
    Wo = p.inp("Wo", (2048, D))
    bo = p.inp("bo", (D, 1))
    w1 = p.inp("w1", (D, 4096))
    w2 = p.inp("w2", (4096, D))
    ln_g = p.inp("ln_g", (D, 1))
    ln_b = p.inp("ln_b", (D, 1))
    W_in1 = p.inp("W_in1", (D, 2048))
    x2T = p.out("x2T", (D, TS))
    xzT = p.out("xzT", (2048, TS))

    nf = 8
    acc = [p.sb((128, CW)) for _ in range(nf)]
    sq = [p.sb((128, CW)) for _ in range(nf)]
    abf = [p.sb((128, CW), BF16) for _ in range(16)]
    hbf = [p.sb((128, CW), BF16) for _ in range(32)]
    wst = p.sb((128, 4096))
    wbf = p.sb((128, 32 * 1024), BF16)
    bocol = [p.sb((128, 1)) for _ in range(nf)]
    gcol = [p.sb((128, 1)) for _ in range(nf)]
    bcol = [p.sb((128, 1)) for _ in range(nf)]
    ones_col = p.sb((128, 1)); ones_row = p.sb((1, 128))
    row1 = p.sb((1, CW)); row2 = p.sb((1, CW))
    ps_g = [p.ps((128, CW)) for _ in range(4)]
    ps_s1 = p.ps((1, CW)); ps_s2 = p.ps((1, CW))
    ps_b1 = p.ps((128, CW)); ps_b2 = p.ps((128, CW))
    ot = p.sb((128, CW))

    p.vector(lambda v: (v.memset(ones_col[:], 1.0), v.memset(ones_row[:], 1.0)))
    p.load([(bocol[f][:], bo[f * 128:(f + 1) * 128, :]) for f in range(nf)]
           + [(gcol[f][:], ln_g[f * 128:(f + 1) * 128, :]) for f in range(nf)]
           + [(bcol[f][:], ln_b[f * 128:(f + 1) * 128, :]) for f in range(nf)])

    def ln_plain(gam=None, bet=None):
        p.scalar(lambda s: [s.activation(out=sq[f][:], in_=acc[f][:], func=AF.Square)
                            for f in range(nf)])
        def stats(te):
            for f in range(nf):
                te.matmul(ps_s1[:], ones_col[:], acc[f][:], start=(f == 0), stop=(f == nf - 1))
            for f in range(nf):
                te.matmul(ps_s2[:], ones_col[:], sq[f][:], start=(f == 0), stop=(f == nf - 1))
        p.tensor(stats)
        def rowops(v):
            v.tensor_scalar_mul(row1[:], ps_s1[0:1, :], 1.0 / 1024.0)
            v.tensor_scalar_mul(row2[:], ps_s2[0:1, :], 1.0 / 1024.0)
            v.tensor_mul(sq[0][0:1, :], row1[:], row1[:])
            v.tensor_sub(row2[:], row2[:], sq[0][0:1, :])
            v.tensor_scalar_add(row2[:], row2[:], 1e-5)
        p.vector(rowops)
        p.scalar(lambda s: (s.activation(out=row2[:], in_=row2[:], func=AF.Ln),
                            s.activation(out=row2[:], in_=row2[:], func=AF.Exp, scale=-0.5)))
        def bc(te):
            te.matmul(ps_b1[:], ones_row[:], row1[:], start=True, stop=True)
            te.matmul(ps_b2[:], ones_row[:], row2[:], start=True, stop=True)
        p.tensor(bc)
        def norm(v):
            for f in range(nf):
                v.tensor_sub(acc[f][:], acc[f][:], ps_b1[:])
                v.tensor_mul(acc[f][:], acc[f][:], ps_b2[:])
                if gam is not None:
                    v.tensor_scalar(acc[f][:], acc[f][:], gam[f][:], bet[f][:], ALU.mult, ALU.add)
        p.vector(norm)
    p.ln_plain = ln_plain

    for ci in range(NCH):
        c0 = ci * CW
        p.load([(sq[f][:], yT[f * 128:(f + 1) * 128, c0:c0 + CW]) for f in range(8)])
        p.scalar(lambda s: [s.copy(out=abf[f][:], in_=sq[f][:]) for f in range(8)])
        p.load([(sq[f][:], yT[(8 + f) * 128:(9 + f) * 128, c0:c0 + CW]) for f in range(8)])
        p.scalar(lambda s: [s.copy(out=abf[8 + f][:], in_=sq[f][:]) for f in range(8)])
        emit_weight_load(p, Wo, 0, 16, D, wst, wbf)
        p.load([(acc[f][:], xT[f * 128:(f + 1) * 128, c0:c0 + CW]) for f in range(nf)])
        for mg in range(2):
            mlist = [mg * 4 + i for i in range(4)]
            emit_matmul(p, wbf, D, 16, abf, ps_g, mlist, CW)
            def evac(v, mlist=mlist):
                for mi, m in enumerate(mlist):
                    v.tensor_scalar_add(sq[m][:], ps_g[mi][:], bocol[m][:])
                    v.scalar_tensor_tensor(acc[m][:], acc[m][:], 2.0, sq[m][:], ALU.mult, ALU.add)
            p.vector(evac)
        ln_plain()
        x1 = [p_tile for p_tile in acc]
        p.vector(lambda v: [v.tensor_copy(sq[f][:], acc[f][:]) for f in range(nf)])
        p.scalar(lambda s: [s.copy(out=abf[f][:], in_=acc[f][:]) for f in range(nf)])
        emit_weight_load(p, w1, 0, 8, 4096, wst, wbf)
        for mg in range(8):
            mlist = [mg * 4 + i for i in range(4)]
            emit_matmul(p, wbf, 4096, 8, abf, ps_g, mlist, CW)
            def evac(s, mlist=mlist):
                for mi, m in enumerate(mlist):
                    s.activation(out=hbf[m][:], in_=ps_g[mi][:], func=AF.Relu)
            p.scalar(evac)
        emit_weight_load(p, w2, 0, 32, D, wst, wbf)
        for mg in range(2):
            mlist = [mg * 4 + i for i in range(4)]
            emit_matmul(p, wbf, D, 32, hbf, ps_g, mlist, CW)
            def evac(v, mlist=mlist):
                for mi, m in enumerate(mlist):
                    v.tensor_add(acc[m][:], sq[m][:], ps_g[mi][:])
            p.vector(evac)
        ln_plain()
        for f in range(nf):
            p.store([(x2T[f * 128:(f + 1) * 128, c0:c0 + CW], acc[f][:])])
        ln_plain(gam=gcol, bet=bcol)
        p.scalar(lambda s: [s.copy(out=abf[f][:], in_=acc[f][:]) for f in range(nf)])
        emit_weight_load(p, W_in1, 0, 8, 2048, wst, wbf)
        for mg in range(4):
            mlist = [mg * 4 + i for i in range(4)]
            emit_matmul(p, wbf, 2048, 8, abf, ps_g, mlist, CW)
            for mi, m in enumerate(mlist):
                p.vector(lambda v, mi=mi: v.tensor_copy(ot[:], ps_g[mi][:]))
                p.store([(xzT[m * 128:(m + 1) * 128, c0:c0 + CW], ot[:])])
    return p.finish()


def bld_k5():
    p = Prog("k5")
    xzS = p.inp("xzS", (D, TS + 3))
    conv_w = p.inp("conv_w", (4 * D, 1))
    conv_b = p.inp("conv_b", (D, 1))
    dt_w = p.inp("dt_w", (D, D))
    dt_b = p.inp("dt_b", (D, 1))
    B_w = p.inp("B_w", (D, 16))
    C_w = p.inp("C_w", (D, 16))
    logA = p.inp("logA", (D, 1))
    xsT = p.out("xsT", (D, TS))
    decT = p.out("decT", (D, TS))
    dtxT = p.out("dtxT", (D, TS))
    BmT = p.out("BmT", (16, TS))
    CmT = p.out("CmT", (16, TS))

    nf = 8
    pt = p.sb((128, CW + 3))
    c1 = p.sb((128, CW)); c2 = p.sb((128, CW))
    xs = [p.sb((128, CW)) for _ in range(nf)]
    xsb = [p.sb((128, CW), BF16) for _ in range(nf)]
    cwt = [p.sb((128, 1)) for _ in range(4 * nf)]
    cbt = [p.sb((128, 1)) for _ in range(nf)]
    dtb = [p.sb((128, 1)) for _ in range(nf)]
    Acol = [p.sb((128, 1)) for _ in range(nf)]
    wst = p.sb((128, D))
    wbf = p.sb((128, 8 * D), BF16)
    wbcbf = p.sb((128, 8 * 32), BF16)
    ps_g = [p.ps((128, CW)) for _ in range(4)]
    ps_bc = p.ps((32, CW))
    t1 = p.sb((128, CW)); t2 = p.sb((128, CW)); t3 = p.sb((128, CW))
    obc = p.sb((32, CW))

    p.load([(cwt[4 * f + j][:], conv_w[j * D + f * 128:j * D + (f + 1) * 128, :])
            for f in range(nf) for j in range(4)]
           + [(cbt[f][:], conv_b[f * 128:(f + 1) * 128, :]) for f in range(nf)]
           + [(dtb[f][:], dt_b[f * 128:(f + 1) * 128, :]) for f in range(nf)]
           + [(Acol[f][:], logA[f * 128:(f + 1) * 128, :]) for f in range(nf)])
    p.scalar(lambda s: [s.activation(out=Acol[f][:], in_=Acol[f][:], func=AF.Exp)
                        for f in range(nf)])
    p.vector(lambda v: [v.tensor_scalar_mul(Acol[f][:], Acol[f][:], -1.0) for f in range(nf)])
    emit_weight_load(p, dt_w, 0, 8, D, wst, wbf)
    for j in range(8):
        p.load([(wst[:, 0:16], B_w[j * 128:(j + 1) * 128, :]),
                (wst[:, 16:32], C_w[j * 128:(j + 1) * 128, :])])
        p.scalar(lambda s, j=j: s.copy(out=wbcbf[:, j * 32:(j + 1) * 32], in_=wst[:, 0:32]))

    for ci in range(NCH):
        c0 = ci * CW
        for f in range(nf):
            p.load([(pt[:], xzS[f * 128:(f + 1) * 128, c0:c0 + CW + 3])])
            def conv(v, f=f):
                v.tensor_scalar(c1[:], pt[:, 3:CW + 3], cwt[4 * f + 3][:], cbt[f][:], ALU.mult, ALU.add)
                v.scalar_tensor_tensor(c2[:], pt[:, 2:CW + 2], cwt[4 * f + 2][:], c1[:], ALU.mult, ALU.add)
                v.scalar_tensor_tensor(c1[:], pt[:, 1:CW + 1], cwt[4 * f + 1][:], c2[:], ALU.mult, ALU.add)
                v.scalar_tensor_tensor(c2[:], pt[:, 0:CW], cwt[4 * f + 0][:], c1[:], ALU.mult, ALU.add)
            p.vector(conv)
            p.scalar(lambda s, f=f: s.activation(out=xs[f][:], in_=c2[:], func=AF.Silu))
            p.scalar(lambda s, f=f: s.copy(out=xsb[f][:], in_=xs[f][:]))
            p.store([(xsT[f * 128:(f + 1) * 128, c0:c0 + CW], xs[f][:])])
        for mg in range(2):
            mlist = [mg * 4 + i for i in range(4)]
            emit_matmul(p, wbf, D, 8, xsb, ps_g, mlist, CW)
            for mi, m in enumerate(mlist):
                def sp(s, mi=mi, m=m):
                    s.activation(out=t1[:], in_=ps_g[mi][:], func=AF.Abs, bias=dtb[m][:])
                    s.activation(out=t1[:], in_=t1[:], func=AF.Exp, scale=-1.0)
                    s.activation(out=t1[:], in_=t1[:], func=AF.Ln, bias=1.0)
                p.scalar(sp)
                def sp2(v, mi=mi, m=m):
                    v.tensor_scalar_add(t2[:], ps_g[mi][:], dtb[m][:])
                    v.tensor_scalar_max(t2[:], t2[:], 0.0)
                    v.tensor_add(t2[:], t2[:], t1[:])
                    v.tensor_mul(t3[:], t2[:], xs[m][:])
                    v.tensor_scalar_mul(t1[:], t2[:], Acol[m][:])
                    v.tensor_scalar_min(t1[:], t1[:], 0.0)
                p.vector(sp2)
                p.scalar(lambda s: s.activation(out=t1[:], in_=t1[:], func=AF.Exp))
                p.store([(dtxT[m * 128:(m + 1) * 128, c0:c0 + CW], t3[:]),
                         (decT[m * 128:(m + 1) * 128, c0:c0 + CW], t1[:])])
        def mmbc(te):
            for k in range(8):
                te.matmul(ps_bc[:], wbcbf[:, k * 32:(k + 1) * 32], xsb[k][:],
                          start=(k == 0), stop=(k == 7))
        p.tensor(mmbc)
        p.vector(lambda v: v.tensor_copy(obc[:], ps_bc[:]))
        p.store([(BmT[:, c0:c0 + CW], obc[0:16, :]),
                 (CmT[:, c0:c0 + CW], obc[16:32, :])])
    return p.finish()


def bld_k6():
    p = Prog("k6")
    decR = p.inp("decR", (128, N))
    dtxR = p.inp("dtxR", (128, N))
    BmF = p.inp("BmF", (16, N))
    CmF = p.inp("CmF", (16, N))
    yR = p.out("yR", (128, N))
    dec = p.sb((128, T)); dtx = p.sb((128, T))
    Brow = p.sb((1, T))
    Crow = p.sb((1, T))
    bb = p.sb((128, T))
    hs = p.sb((128, T))
    yacc = p.sb((128, T))
    ones_row = p.sb((1, 128))
    ps_bb = p.ps((128, 2048))
    p.vector(lambda v: v.memset(ones_row[:], 1.0))

    def bcast(row):
        for hh in range(2):
            o0 = hh * 2048
            def mm(te, o0=o0, row=row):
                for j in range(4):
                    te.matmul(ps_bb[:, j * CW:(j + 1) * CW], ones_row[:],
                              row[:, o0 + j * CW:o0 + (j + 1) * CW], start=True, stop=True)
            p.tensor(mm)
            p.vector(lambda v, o0=o0: v.tensor_copy(bb[:, o0:o0 + 2048], ps_bb[:]))

    for b in range(B):
        cc0 = b * T
        p.load([(dec[:], decR[:, cc0:cc0 + T]), (dtx[:], dtxR[:, cc0:cc0 + T])])
        p.vector(lambda v: v.memset(yacc[:], 0.0))
        for s in range(16):
            p.load([(Brow[:], BmF[s:s + 1, cc0:cc0 + T]),
                    (Crow[:], CmF[s:s + 1, cc0:cc0 + T])])
            bcast(Brow)
            p.vector(lambda v: (v.tensor_mul(bb[:], bb[:], dtx[:]),
                                v.tensor_tensor_scan(hs[:], dec[:], bb[:], 0.0, ALU.mult, ALU.add)))
            bcast(Crow)
            p.vector(lambda v: (v.tensor_mul(bb[:], bb[:], hs[:]),
                                v.tensor_add(yacc[:], yacc[:], bb[:])))
        p.store([(yR[:, cc0:cc0 + T], yacc[:])])
    return p.finish()


def bld_k7():
    p = Prog("k7")
    ysT = p.inp("ysT", (D, TS))
    xsT = p.inp("xsT", (D, TS))
    zT = p.inp("zT", (D, TS))
    Dp = p.inp("Dp", (D, 1))
    W_out = p.inp("W_out", (D, D))
    x2T = p.inp("x2T", (D, TS))
    x3T = p.out("x3T", (D, TS))
    nf = 8
    acc = [p.sb((128, CW)) for _ in range(nf)]
    sq = [p.sb((128, CW)) for _ in range(nf)]
    abf = [p.sb((128, CW), BF16) for _ in range(nf)]
    t1 = p.sb((128, CW)); t2 = p.sb((128, CW))
    Dcol = [p.sb((128, 1)) for _ in range(nf)]
    wst = p.sb((128, D))
    wbf = p.sb((128, 8 * D), BF16)
    ones_col = p.sb((128, 1)); ones_row = p.sb((1, 128))
    row1 = p.sb((1, CW)); row2 = p.sb((1, CW))
    ps_g = [p.ps((128, CW)) for _ in range(4)]
    ps_s1 = p.ps((1, CW)); ps_s2 = p.ps((1, CW))
    ps_b1 = p.ps((128, CW)); ps_b2 = p.ps((128, CW))

    p.vector(lambda v: (v.memset(ones_col[:], 1.0), v.memset(ones_row[:], 1.0)))
    p.load([(Dcol[f][:], Dp[f * 128:(f + 1) * 128, :]) for f in range(nf)])
    emit_weight_load(p, W_out, 0, 8, D, wst, wbf)

    def ln_plain():
        p.scalar(lambda s: [s.activation(out=sq[f][:], in_=acc[f][:], func=AF.Square)
                            for f in range(nf)])
        def stats(te):
            for f in range(nf):
                te.matmul(ps_s1[:], ones_col[:], acc[f][:], start=(f == 0), stop=(f == nf - 1))
            for f in range(nf):
                te.matmul(ps_s2[:], ones_col[:], sq[f][:], start=(f == 0), stop=(f == nf - 1))
        p.tensor(stats)
        def rowops(v):
            v.tensor_scalar_mul(row1[:], ps_s1[0:1, :], 1.0 / 1024.0)
            v.tensor_scalar_mul(row2[:], ps_s2[0:1, :], 1.0 / 1024.0)
            v.tensor_mul(sq[0][0:1, :], row1[:], row1[:])
            v.tensor_sub(row2[:], row2[:], sq[0][0:1, :])
            v.tensor_scalar_add(row2[:], row2[:], 1e-5)
        p.vector(rowops)
        p.scalar(lambda s: (s.activation(out=row2[:], in_=row2[:], func=AF.Ln),
                            s.activation(out=row2[:], in_=row2[:], func=AF.Exp, scale=-0.5)))
        def bc(te):
            te.matmul(ps_b1[:], ones_row[:], row1[:], start=True, stop=True)
            te.matmul(ps_b2[:], ones_row[:], row2[:], start=True, stop=True)
        p.tensor(bc)
        def norm(v):
            for f in range(nf):
                v.tensor_sub(acc[f][:], acc[f][:], ps_b1[:])
                v.tensor_mul(acc[f][:], acc[f][:], ps_b2[:])
        p.vector(norm)

    for ci in range(NCH):
        c0 = ci * CW
        for f in range(nf):
            p.load([(t1[:], ysT[f * 128:(f + 1) * 128, c0:c0 + CW]),
                    (t2[:], xsT[f * 128:(f + 1) * 128, c0:c0 + CW]),
                    (sq[f][:], zT[f * 128:(f + 1) * 128, c0:c0 + CW])])
            def comb(v, f=f):
                v.scalar_tensor_tensor(t1[:], t2[:], Dcol[f][:], t1[:], ALU.mult, ALU.add)
            p.vector(comb)
            p.scalar(lambda s, f=f: s.activation(out=sq[f][:], in_=sq[f][:], func=AF.Silu))
            p.vector(lambda v, f=f: v.tensor_mul(t1[:], t1[:], sq[f][:]))
            p.scalar(lambda s, f=f: s.copy(out=abf[f][:], in_=t1[:]))
        p.load([(acc[f][:], x2T[f * 128:(f + 1) * 128, c0:c0 + CW]) for f in range(nf)])
        for mg in range(2):
            mlist = [mg * 4 + i for i in range(4)]
            emit_matmul(p, wbf, D, 8, abf, ps_g, mlist, CW)
            def evac(v, mlist=mlist):
                for mi, m in enumerate(mlist):
                    v.scalar_tensor_tensor(acc[m][:], acc[m][:], 2.0, ps_g[mi][:], ALU.mult, ALU.add)
            p.vector(evac)
        ln_plain()
        for f in range(nf):
            p.store([(x3T[f * 128:(f + 1) * 128, c0:c0 + CW], acc[f][:])])
    return p.finish()


def bld_k8():
    p = Prog("k8")
    x3S = p.inp("x3S", (D, TS + 1))
    W_rT = p.inp("W_rT", (D, D))
    W_kT = p.inp("W_kT", (D, D))
    W_vT = p.inp("W_vT", (D, D))
    mu_r = p.inp("mu_r", (D, 1))
    mu_k = p.inp("mu_k", (D, 1))
    WzT0 = p.inp("WzT0", (D, D))
    bz0 = p.inp("bz0", (D, 1))
    WhT0 = p.inp("WhT0", (D, D))
    bh0 = p.inp("bh0", (D, 1))
    x4T = p.out("x4T", (D, TS))
    coefT = p.out("coefT", (D, TS))
    winT = p.out("winT", (D, TS))

    nf = 8
    acc = [p.sb((128, CW)) for _ in range(nf)]
    sq = [p.sb((128, CW)) for _ in range(nf)]
    xrb = [p.sb((128, CW), BF16) for _ in range(nf)]
    xkb = [p.sb((128, CW), BF16) for _ in range(nf)]
    kab = [p.sb((128, CW), BF16) for _ in range(nf)]
    x4b = [p.sb((128, CW), BF16) for _ in range(nf)]
    xt = p.sb((128, CW + 1))
    t1 = p.sb((128, CW)); t2 = p.sb((128, CW))
    mur = [p.sb((128, 1)) for _ in range(nf)]
    muk = [p.sb((128, 1)) for _ in range(nf)]
    bz0c = [p.sb((128, 1)) for _ in range(nf)]
    bh0c = [p.sb((128, 1)) for _ in range(nf)]
    wst = p.sb((128, D))
    wbf = p.sb((128, 8 * D), BF16)
    ones_col = p.sb((128, 1)); ones_row = p.sb((1, 128))
    row1 = p.sb((1, CW)); row2 = p.sb((1, CW))
    ps_g = [p.ps((128, CW)) for _ in range(4)]
    ps_s1 = p.ps((1, CW)); ps_s2 = p.ps((1, CW))
    ps_b1 = p.ps((128, CW)); ps_b2 = p.ps((128, CW))

    p.vector(lambda v: (v.memset(ones_col[:], 1.0), v.memset(ones_row[:], 1.0)))
    p.load([(mur[f][:], mu_r[f * 128:(f + 1) * 128, :]) for f in range(nf)]
           + [(muk[f][:], mu_k[f * 128:(f + 1) * 128, :]) for f in range(nf)]
           + [(bz0c[f][:], bz0[f * 128:(f + 1) * 128, :]) for f in range(nf)]
           + [(bh0c[f][:], bh0[f * 128:(f + 1) * 128, :]) for f in range(nf)])

    def ln_plain():
        p.scalar(lambda s: [s.activation(out=sq[f][:], in_=acc[f][:], func=AF.Square)
                            for f in range(nf)])
        def stats(te):
            for f in range(nf):
                te.matmul(ps_s1[:], ones_col[:], acc[f][:], start=(f == 0), stop=(f == nf - 1))
            for f in range(nf):
                te.matmul(ps_s2[:], ones_col[:], sq[f][:], start=(f == 0), stop=(f == nf - 1))
        p.tensor(stats)
        def rowops(v):
            v.tensor_scalar_mul(row1[:], ps_s1[0:1, :], 1.0 / 1024.0)
            v.tensor_scalar_mul(row2[:], ps_s2[0:1, :], 1.0 / 1024.0)
            v.tensor_mul(sq[0][0:1, :], row1[:], row1[:])
            v.tensor_sub(row2[:], row2[:], sq[0][0:1, :])
            v.tensor_scalar_add(row2[:], row2[:], 1e-5)
        p.vector(rowops)
        p.scalar(lambda s: (s.activation(out=row2[:], in_=row2[:], func=AF.Ln),
                            s.activation(out=row2[:], in_=row2[:], func=AF.Exp, scale=-0.5)))
        def bc(te):
            te.matmul(ps_b1[:], ones_row[:], row1[:], start=True, stop=True)
            te.matmul(ps_b2[:], ones_row[:], row2[:], start=True, stop=True)
        p.tensor(bc)
        def norm(v):
            for f in range(nf):
                v.tensor_sub(acc[f][:], acc[f][:], ps_b1[:])
                v.tensor_mul(acc[f][:], acc[f][:], ps_b2[:])
        p.vector(norm)

    for ci in range(NCH):
        c0 = ci * CW
        for f in range(nf):
            p.load([(xt[:], x3S[f * 128:(f + 1) * 128, c0:c0 + CW + 1])])
            def mix(v, f=f):
                v.tensor_sub(t1[:], xt[:, 1:CW + 1], xt[:, 0:CW])
                v.scalar_tensor_tensor(t2[:], t1[:], mur[f][:], xt[:, 0:CW], ALU.mult, ALU.add)
            p.vector(mix)
            p.scalar(lambda s, f=f: s.copy(out=xrb[f][:], in_=t2[:]))
            def mix2(v, f=f):
                v.scalar_tensor_tensor(t2[:], t1[:], muk[f][:], xt[:, 0:CW], ALU.mult, ALU.add)
            p.vector(mix2)
            p.scalar(lambda s, f=f: s.copy(out=xkb[f][:], in_=t2[:]))
            p.vector(lambda v, f=f: v.tensor_copy(acc[f][:], xt[:, 1:CW + 1]))
        emit_weight_load(p, W_kT, 0, 8, D, wst, wbf)
        for mg in range(2):
            mlist = [mg * 4 + i for i in range(4)]
            emit_matmul(p, wbf, D, 8, xkb, ps_g, mlist, CW)
            def evac(s, mlist=mlist):
                for mi, m in enumerate(mlist):
                    s.activation(out=t1[:], in_=ps_g[mi][:], func=AF.Relu)
                    s.activation(out=kab[m][:], in_=t1[:], func=AF.Square)
            p.scalar(evac)
        emit_weight_load(p, W_rT, 0, 8, D, wst, wbf)
        for mg in range(2):
            mlist = [mg * 4 + i for i in range(4)]
            emit_matmul(p, wbf, D, 8, xrb, ps_g, mlist, CW)
            def evac(s, mlist=mlist):
                for mi, m in enumerate(mlist):
                    s.activation(out=sq[m][:], in_=ps_g[mi][:], func=AF.Sigmoid)
            p.scalar(evac)
        emit_weight_load(p, W_vT, 0, 8, D, wst, wbf)
        for mg in range(2):
            mlist = [mg * 4 + i for i in range(4)]
            emit_matmul(p, wbf, D, 8, kab, ps_g, mlist, CW)
            def evac(v, mlist=mlist):
                for mi, m in enumerate(mlist):
                    v.tensor_mul(t1[:], sq[m][:], ps_g[mi][:])
                    v.tensor_add(acc[m][:], acc[m][:], t1[:])
            p.vector(evac)
        ln_plain()
        for f in range(nf):
            p.store([(x4T[f * 128:(f + 1) * 128, c0:c0 + CW], acc[f][:])])
        p.scalar(lambda s: [s.copy(out=x4b[f][:], in_=acc[f][:]) for f in range(nf)])
        emit_weight_load(p, WzT0, 0, 8, D, wst, wbf)
        for mg in range(2):
            mlist = [mg * 4 + i for i in range(4)]
            emit_matmul(p, wbf, D, 8, x4b, ps_g, mlist, CW)
            def evac(s, mlist=mlist):
                for mi, m in enumerate(mlist):
                    s.activation(out=sq[m][:], in_=ps_g[mi][:], func=AF.Sigmoid, bias=bz0c[m][:])
            p.scalar(evac)
            for mi, m in enumerate(mlist):
                p.vector(lambda v, m=m: v.tensor_scalar(
                    acc[m][:], sq[m][:], -1.0, 1.0, ALU.mult, ALU.add))
                p.store([(coefT[m * 128:(m + 1) * 128, c0:c0 + CW], acc[m][:])])
        emit_weight_load(p, WhT0, 0, 8, D, wst, wbf)
        for mg in range(2):
            mlist = [mg * 4 + i for i in range(4)]
            emit_matmul(p, wbf, D, 8, x4b, ps_g, mlist, CW)
            for mi, m in enumerate(mlist):
                def gfun(v, mi=mi, m=m):
                    v.tensor_scalar_add(t1[:], ps_g[mi][:], bh0c[m][:])
                    v.tensor_scalar_min(t2[:], t1[:], 0.0)
                    v.tensor_scalar_max(t1[:], t1[:], 0.0)
                p.vector(gfun)
                p.scalar(lambda s: s.activation(out=t2[:], in_=t2[:], func=AF.Sigmoid))
                def gfun2(v, m=m):
                    v.tensor_add(t1[:], t1[:], t2[:])
                    v.tensor_mul(t1[:], t1[:], sq[m][:])
                p.vector(gfun2)
                p.store([(winT[m * 128:(m + 1) * 128, c0:c0 + CW], t1[:])])
    return p.finish()


def bld_k9():
    p = Prog("k9")
    aR = p.inp("aR", (128, N))
    vR = p.inp("vR", (128, N))
    hR = p.out("hR", (128, N))
    at = p.sb((128, T)); vt = p.sb((128, T)); ht = p.sb((128, T))
    for b in range(B):
        cc0 = b * T
        p.load([(at[:], aR[:, cc0:cc0 + T]), (vt[:], vR[:, cc0:cc0 + T])])
        p.vector(lambda v: v.tensor_tensor_scan(ht[:], at[:], vt[:], 0.5, ALU.mult, ALU.add))
        p.store([(hR[:, cc0:cc0 + T], ht[:])])
    return p.finish()


def bld_k10():
    p = Prog("k10")
    h1T = p.inp("h1T", (D, TS))
    WzT1 = p.inp("WzT1", (D, D))
    bz1 = p.inp("bz1", (D, 1))
    WhT1 = p.inp("WhT1", (D, D))
    bh1 = p.inp("bh1", (D, 1))
    coefT = p.out("coefT", (D, TS))
    winT = p.out("winT", (D, TS))
    nf = 8
    hb = [p.sb((128, CW), BF16) for _ in range(nf)]
    sq = [p.sb((128, CW)) for _ in range(nf)]
    t1 = p.sb((128, CW)); t2 = p.sb((128, CW)); t3 = p.sb((128, CW))
    bzc = [p.sb((128, 1)) for _ in range(nf)]
    bhc = [p.sb((128, 1)) for _ in range(nf)]
    wst = p.sb((128, D))
    wbf = p.sb((128, 8 * D), BF16)
    wbf2 = p.sb((128, 8 * D), BF16)
    ps_g = [p.ps((128, CW)) for _ in range(4)]
    p.load([(bzc[f][:], bz1[f * 128:(f + 1) * 128, :]) for f in range(nf)]
           + [(bhc[f][:], bh1[f * 128:(f + 1) * 128, :]) for f in range(nf)])
    emit_weight_load(p, WzT1, 0, 8, D, wst, wbf)
    emit_weight_load(p, WhT1, 0, 8, D, wst, wbf2)
    for ci in range(NCH):
        c0 = ci * CW
        p.load([(sq[f][:], h1T[f * 128:(f + 1) * 128, c0:c0 + CW]) for f in range(nf)])
        p.scalar(lambda s: [s.copy(out=hb[f][:], in_=sq[f][:]) for f in range(nf)])
        for mg in range(2):
            mlist = [mg * 4 + i for i in range(4)]
            emit_matmul(p, wbf, D, 8, hb, ps_g, mlist, CW)
            for mi, m in enumerate(mlist):
                p.scalar(lambda s, mi=mi, m=m: s.activation(
                    out=sq[m][:], in_=ps_g[mi][:], func=AF.Sigmoid, bias=bzc[m][:]))
                def fix(v, m=m):
                    v.tensor_scalar(t2[:], sq[m][:], -1.0, 1.0, ALU.mult, ALU.add)
                p.vector(fix)
                p.store([(coefT[m * 128:(m + 1) * 128, c0:c0 + CW], t2[:])])
        for mg in range(2):
            mlist = [mg * 4 + i for i in range(4)]
            emit_matmul(p, wbf2, D, 8, hb, ps_g, mlist, CW)
            for mi, m in enumerate(mlist):
                def gfun(v, mi=mi, m=m):
                    v.tensor_scalar_add(t1[:], ps_g[mi][:], bhc[m][:])
                    v.tensor_scalar_min(t2[:], t1[:], 0.0)
                    v.tensor_scalar_max(t1[:], t1[:], 0.0)
                p.vector(gfun)
                p.scalar(lambda s: s.activation(out=t2[:], in_=t2[:], func=AF.Sigmoid))
                def gfun2(v, m=m):
                    v.tensor_add(t1[:], t1[:], t2[:])
                    v.tensor_mul(t1[:], t1[:], sq[m][:])
                p.vector(gfun2)
                p.store([(winT[m * 128:(m + 1) * 128, c0:c0 + CW], t1[:])])
    return p.finish()


def bld_k12b(r):
    p = Prog(f"k12r{r}")
    curT = p.inp("curT", (D, TS))
    x4T = p.inp("x4T", (D, TS)) if r == 0 else None
    Wg = p.inp("Wg", (D, 4))
    bg = p.inp("bg", (1, 4))
    W1 = p.inp("W1", (4 * D, 4096))
    b1 = p.inp("b1", (4 * 4096, 1))
    W2 = p.inp("W2", (4 * 4096, D))
    b2 = p.inp("b2", (4 * D, 1))
    identD = p.inp("identD", (128, 128))
    outT = p.out("outT", (D, TS))
    x5T = p.out("x5T", (D, TS)) if r == 0 else None
    cwD = p.out("cwD", (4, TS))
    cwbD = p.out("cwbD", (128, TS))
    cwM = p.out("cwM", (TS, 4))
    lgM = p.out("lgM", (TS, 4))

    nf = 8
    acc = [p.sb((128, CW)) for _ in range(nf)]
    osb = [p.sb((128, CW)) for _ in range(nf)]
    sq = [p.sb((128, CW)) for _ in range(nf)]
    cb = [p.sb((128, CW), BF16) for _ in range(nf)]
    hbf = [p.sb((128, CW), BF16) for _ in range(16)]
    wst = p.sb((128, 2048))
    wbf1 = p.sb((128, 8 * 2048), BF16)
    wbf2 = p.sb((128, 16 * 1024), BF16)
    wgb = p.sb((128, 8 * 4), BF16)
    bgrow = p.sb((1, 4))
    b1c = [p.sb((128, 1)) for _ in range(16)]
    b2c = [p.sb((128, 1)) for _ in range(nf)]
    cwR = [p.sb((1, CW)) for _ in range(4)]
    cwb = p.sb((128, CW))
    lg = p.sb((128, 4))
    m1 = p.sb((128, 1)); m2c = p.sb((128, 1)); den = p.sb((128, 1))
    identf = p.sb((128, 128))
    ones_col = p.sb((128, 1)); ones_row = p.sb((1, 128))
    row1 = p.sb((1, CW)); row2 = p.sb((1, CW))
    t1 = p.sb((128, CW)); t2 = p.sb((128, CW)); t3 = p.sb((128, CW))
    g4a = p.sb((128, 4)); g4b = p.sb((128, 4))
    bgfull = p.sb((128, 4))
    ps_g = [p.ps((128, CW)) for _ in range(2)]
    ps_s1 = p.ps((1, CW)); ps_s2 = p.ps((1, CW))
    ps_b1 = p.ps((128, CW)); ps_b2 = p.ps((128, CW))
    ps_l = p.ps((128, 4))
    ps_t = p.ps((4, 128))

    p.load([(identf[:], identD), (bgrow[:], bg)])
    p.vector(lambda v: (v.memset(ones_col[:], 1.0), v.memset(ones_row[:], 1.0)))
    p.tensor(lambda te: te.matmul(ps_l[:], ones_row[:], bgrow[:], start=True, stop=True))
    p.vector(lambda v: v.tensor_copy(bgfull[:], ps_l[:]))
    for j in range(8):
        p.load([(wst[:, 0:4], Wg[j * 128:(j + 1) * 128, :])])
        p.scalar(lambda s, j=j: s.copy(out=wgb[:, j * 4:(j + 1) * 4], in_=wst[:, 0:4]))

    def ln_plain():
        p.scalar(lambda s: [s.activation(out=sq[f][:], in_=acc[f][:], func=AF.Square)
                            for f in range(nf)])
        def stats(te):
            for f in range(nf):
                te.matmul(ps_s1[:], ones_col[:], acc[f][:], start=(f == 0), stop=(f == nf - 1))
            for f in range(nf):
                te.matmul(ps_s2[:], ones_col[:], sq[f][:], start=(f == 0), stop=(f == nf - 1))
        p.tensor(stats)
        def rowops(v):
            v.tensor_scalar_mul(row1[:], ps_s1[0:1, :], 1.0 / 1024.0)
            v.tensor_scalar_mul(row2[:], ps_s2[0:1, :], 1.0 / 1024.0)
            v.tensor_mul(sq[0][0:1, :], row1[:], row1[:])
            v.tensor_sub(row2[:], row2[:], sq[0][0:1, :])
            v.tensor_scalar_add(row2[:], row2[:], 1e-5)
        p.vector(rowops)
        p.scalar(lambda s: (s.activation(out=row2[:], in_=row2[:], func=AF.Ln),
                            s.activation(out=row2[:], in_=row2[:], func=AF.Exp, scale=-0.5)))
        def bc(te):
            te.matmul(ps_b1[:], ones_row[:], row1[:], start=True, stop=True)
            te.matmul(ps_b2[:], ones_row[:], row2[:], start=True, stop=True)
        p.tensor(bc)
        def norm(v):
            for f in range(nf):
                v.tensor_sub(acc[f][:], acc[f][:], ps_b1[:])
                v.tensor_mul(acc[f][:], acc[f][:], ps_b2[:])
        p.vector(norm)

    for ci in range(NCH):
        c0 = ci * CW
        p.load([(acc[f][:], curT[f * 128:(f + 1) * 128, c0:c0 + CW]) for f in range(nf)])
        if r == 0:
            p.load([(sq[f][:], x4T[f * 128:(f + 1) * 128, c0:c0 + CW]) for f in range(nf)])
            p.vector(lambda v: [v.tensor_add(acc[f][:], acc[f][:], sq[f][:]) for f in range(nf)])
            ln_plain()
            for f in range(nf):
                p.store([(x5T[f * 128:(f + 1) * 128, c0:c0 + CW], acc[f][:])])
        p.scalar(lambda s: [s.copy(out=cb[f][:], in_=acc[f][:]) for f in range(nf)])
        p.vector(lambda v: [v.memset(osb[f][:], 0.0) for f in range(nf)])
        for sc in range(CW // 128):
            s0 = sc * 128
            def lgmm(te, s0=s0):
                for k in range(8):
                    te.matmul(ps_l[:], cb[k][:, s0:s0 + 128], wgb[:, k * 4:(k + 1) * 4],
                              start=(k == 0), stop=(k == 7))
            p.tensor(lgmm)
            p.vector(lambda v: v.tensor_tensor(lg[:], ps_l[:], bgfull[:], ALU.add))
            p.vector(lambda v: v.reduce_max(m1[:], lg[:], axis=AX))
            p.vector(lambda v: [v.tensor_tensor(g4b[:, e:e + 1], lg[:, e:e + 1], m1[:],
                                                ALU.is_ge) for e in range(4)])
            p.vector(lambda v: v.scalar_tensor_tensor(g4a[:], g4b[:], -60000.0, lg[:],
                                                      ALU.mult, ALU.add))
            p.vector(lambda v: v.reduce_max(m2c[:], g4a[:], axis=AX))
            p.vector(lambda v: [v.tensor_tensor(g4b[:, e:e + 1], lg[:, e:e + 1], m2c[:],
                                                ALU.is_ge) for e in range(4)])
            p.vector(lambda v: v.tensor_scalar(g4a[:], lg[:], 1.0, m1[:],
                                               ALU.mult, ALU.subtract))
            p.scalar(lambda s: s.activation(out=g4a[:], in_=g4a[:], func=AF.Exp))
            p.vector(lambda v: v.tensor_mul(g4a[:], g4a[:], g4b[:]))
            p.vector(lambda v: v.reduce_sum(den[:], g4a[:], axis=AX))
            p.vector(lambda v: v.reciprocal(m1[:], den[:]))
            p.vector(lambda v: v.tensor_scalar_mul(g4a[:], g4a[:], m1[:]))
            p.store([(cwM[c0 + s0:c0 + s0 + 128, :], g4a[:]),
                     (lgM[c0 + s0:c0 + s0 + 128, :], lg[:])])
            for e in range(4):
                p.tensor(lambda te, e=e: te.matmul(
                    ps_t[0:1, :], g4a[:, e:e + 1], identf[:], start=True, stop=True))
                p.vector(lambda v, e=e, s0=s0: v.tensor_copy(cwR[e][:, s0:s0 + 128],
                                                             ps_t[0:1, :]))
        for e in range(4):
            p.store([(cwD[e:e + 1, c0:c0 + CW], cwR[e][:])])
        for e in range(4):
            p.tensor(lambda te, e=e: te.matmul(ps_b1[:], ones_row[:], cwR[e][:],
                                               start=True, stop=True))
            p.vector(lambda v: v.tensor_copy(cwb[:], ps_b1[:]))
            if e == 0:
                p.store([(cwbD[:, c0:c0 + CW], cwb[:])])
            p.load([(b2c[f][:], b2[e * D + f * 128:e * D + (f + 1) * 128, :]) for f in range(nf)])
            for half in range(2):
                h0 = half * 2048
                p.load([(b1c[j][:], b1[e * 4096 + h0 + j * 128: e * 4096 + h0 + (j + 1) * 128, :])
                        for j in range(16)])
                for j in range(8):
                    p.load([(wst[:, 0:2048], W1[e * D + j * 128: e * D + (j + 1) * 128, h0:h0 + 2048])])
                    p.scalar(lambda s, j=j: s.copy(out=wbf1[:, j * 2048:(j + 1) * 2048],
                                                   in_=wst[:, 0:2048]))
                for mg in range(8):
                    mlist = [mg * 2 + i for i in range(2)]
                    emit_matmul(p, wbf1, 2048, 8, cb, ps_g, mlist, CW)
                    for mi, m in enumerate(mlist):
                        def gel(v, mi=mi, m=m):
                            v.tensor_scalar_add(t1[:], ps_g[mi][:], b1c[m][:])
                            v.tensor_mul(t2[:], t1[:], t1[:])
                            v.tensor_mul(t2[:], t2[:], t1[:])
                            v.scalar_tensor_tensor(t2[:], t2[:], GELU_C1, t1[:], ALU.mult, ALU.add)
                        p.vector(gel)
                        p.scalar(lambda s: s.activation(out=t2[:], in_=t2[:], func=AF.Tanh,
                                                        scale=GELU_C0))
                        def gel2(v, m=m):
                            v.tensor_scalar(t2[:], t2[:], 0.5, 0.5, ALU.mult, ALU.add)
                            v.tensor_mul(t2[:], t2[:], t1[:])
                        p.vector(gel2)
                        p.scalar(lambda s, m=m: s.copy(out=hbf[m][:], in_=t2[:]))
                for j in range(16):
                    p.load([(wst[:, 0:D], W2[e * 4096 + h0 + j * 128: e * 4096 + h0 + (j + 1) * 128, :])])
                    p.scalar(lambda s, j=j: s.copy(out=wbf2[:, j * D:(j + 1) * D], in_=wst[:, 0:D]))
                for mg in range(4):
                    mlist = [mg * 2 + i for i in range(2)]
                    emit_matmul(p, wbf2, D, 16, hbf, ps_g, mlist, CW)
                    def evac(v, mlist=mlist, half=half, e=e):
                        for mi, m in enumerate(mlist):
                            if half == 0:
                                v.tensor_scalar_add(t1[:], ps_g[mi][:], b2c[m][:])
                            else:
                                v.tensor_copy(t1[:], ps_g[mi][:])
                            v.tensor_mul(t1[:], t1[:], cwb[:])
                            v.tensor_add(osb[m][:], osb[m][:], t1[:])
                    p.vector(evac)
        p.vector(lambda v: [v.tensor_add(acc[f][:], acc[f][:], osb[f][:]) for f in range(nf)])
        for f in range(nf):
            p.store([(outT[f * 128:(f + 1) * 128, c0:c0 + CW], acc[f][:])])
    return p.finish()


def bld_k12q():
    p = Prog("k12q")
    curT = p.inp("curT", (D, TS))
    x5Ti = p.inp("x5Ti", (D, TS))
    WqkvT = p.inp("WqkvT", (D, 3 * D))
    bqkv = p.inp("bqkv", (3 * D, 1))
    qpT = p.out("qpT", (D, TS))
    kpT = p.out("kpT", (D, TS))
    vT = p.out("vT", (D, TS))
    x6To = p.out("x6To", (D, TS))
    nf = 8
    xb = [p.sb((128, CW), BF16) for _ in range(nf)]
    acc = [p.sb((128, CW)) for _ in range(nf)]
    sq = [p.sb((128, CW)) for _ in range(nf)]
    t1 = p.sb((128, CW)); t2 = p.sb((128, CW))
    bcols = [p.sb((128, 1)) for _ in range(24)]
    wst = p.sb((128, 3 * D))
    wbf = p.sb((128, 8 * 3 * D), BF16)
    ones_col = p.sb((128, 1)); ones_row = p.sb((1, 128))
    row1 = p.sb((1, CW)); row2 = p.sb((1, CW))
    ps_g = [p.ps((128, CW)) for _ in range(2)]
    ps_s1 = p.ps((1, CW)); ps_s2 = p.ps((1, CW))
    ps_b1 = p.ps((128, CW)); ps_b2 = p.ps((128, CW))
    p.vector(lambda v: (v.memset(ones_col[:], 1.0), v.memset(ones_row[:], 1.0)))
    p.load([(bcols[m][:], bqkv[m * 128:(m + 1) * 128, :]) for m in range(24)])
    emit_weight_load(p, WqkvT, 0, 8, 3 * D, wst, wbf)

    def ln_plain():
        p.scalar(lambda s: [s.activation(out=sq[f][:], in_=acc[f][:], func=AF.Square)
                            for f in range(nf)])
        def stats(te):
            for f in range(nf):
                te.matmul(ps_s1[:], ones_col[:], acc[f][:], start=(f == 0), stop=(f == nf - 1))
            for f in range(nf):
                te.matmul(ps_s2[:], ones_col[:], sq[f][:], start=(f == 0), stop=(f == nf - 1))
        p.tensor(stats)
        def rowops(v):
            v.tensor_scalar_mul(row1[:], ps_s1[0:1, :], 1.0 / 1024.0)
            v.tensor_scalar_mul(row2[:], ps_s2[0:1, :], 1.0 / 1024.0)
            v.tensor_mul(sq[0][0:1, :], row1[:], row1[:])
            v.tensor_sub(row2[:], row2[:], sq[0][0:1, :])
            v.tensor_scalar_add(row2[:], row2[:], 1e-5)
        p.vector(rowops)
        p.scalar(lambda s: (s.activation(out=row2[:], in_=row2[:], func=AF.Ln),
                            s.activation(out=row2[:], in_=row2[:], func=AF.Exp, scale=-0.5)))
        def bc(te):
            te.matmul(ps_b1[:], ones_row[:], row1[:], start=True, stop=True)
            te.matmul(ps_b2[:], ones_row[:], row2[:], start=True, stop=True)
        p.tensor(bc)
        def norm(v):
            for f in range(nf):
                v.tensor_sub(acc[f][:], acc[f][:], ps_b1[:])
                v.tensor_mul(acc[f][:], acc[f][:], ps_b2[:])
        p.vector(norm)

    for ci in range(NCH):
        c0 = ci * CW
        p.load([(acc[f][:], curT[f * 128:(f + 1) * 128, c0:c0 + CW]) for f in range(nf)])
        p.load([(sq[f][:], x5Ti[f * 128:(f + 1) * 128, c0:c0 + CW]) for f in range(nf)])
        p.vector(lambda v: [v.tensor_add(acc[f][:], acc[f][:], sq[f][:]) for f in range(nf)])
        ln_plain()
        for f in range(nf):
            p.store([(x6To[f * 128:(f + 1) * 128, c0:c0 + CW], acc[f][:])])
        p.scalar(lambda s: [s.copy(out=xb[f][:], in_=acc[f][:]) for f in range(nf)])
        for mg in range(12):
            mlist = [mg * 2 + i for i in range(2)]
            emit_matmul(p, wbf, 3 * D, 8, xb, ps_g, mlist, CW)
            for mi, m in enumerate(mlist):
                if m < 8:
                    def ev(v, mi=mi, m=m):
                        v.tensor_scalar_add(t1[:], ps_g[mi][:], bcols[m][:])
                        v.tensor_scalar_min(t2[:], t1[:], 0.0)
                        v.tensor_scalar_max(t1[:], t1[:], 0.0)
                    p.vector(ev)
                    p.scalar(lambda s: s.activation(out=t2[:], in_=t2[:], func=AF.Exp))
                    def ev2(v):
                        v.tensor_add(t1[:], t1[:], t2[:])
                        v.tensor_scalar(t1[:], t1[:], 0.0001, 0.125, ALU.add, ALU.mult)
                    p.vector(ev2)
                    p.store([(qpT[m * 128:(m + 1) * 128, c0:c0 + CW], t1[:])])
                elif m < 16:
                    def ev(v, mi=mi, m=m):
                        v.tensor_scalar_add(t1[:], ps_g[mi][:], bcols[m][:])
                        v.tensor_scalar_min(t2[:], t1[:], 0.0)
                        v.tensor_scalar_max(t1[:], t1[:], 0.0)
                    p.vector(ev)
                    p.scalar(lambda s: s.activation(out=t2[:], in_=t2[:], func=AF.Exp))
                    def ev2(v):
                        v.tensor_add(t1[:], t1[:], t2[:])
                        v.tensor_scalar_add(t1[:], t1[:], 0.0001)
                    p.vector(ev2)
                    p.store([(kpT[(m - 8) * 128:(m - 7) * 128, c0:c0 + CW], t1[:])])
                else:
                    p.vector(lambda v, mi=mi, m=m: v.tensor_scalar_add(t1[:], ps_g[mi][:], bcols[m][:]))
                    p.store([(vT[(m - 16) * 128:(m - 15) * 128, c0:c0 + CW], t1[:])])
    return p.finish()


def bld_k13():
    p = Prog("k13")
    qR = p.inp("qR", (128, N))
    kR = p.inp("kR", (128, N))
    vR = p.inp("vR", (128, N))
    trilD = p.inp("trilD", (128, 128))
    identD = p.inp("identD", (128, 128))
    aR = p.out("aR", (128, N))
    CK = 128
    qt = p.sb((64, T)); kt = p.sb((64, T)); vt = p.sb((64, T))
    qb = p.sb((64, T), BF16); kb = p.sb((64, T), BF16); vb = p.sb((64, T), BF16)
    tril = p.sb((128, 128))
    identf = p.sb((128, 128))
    identb = p.sb((128, 128), BF16)
    S_sb = p.sb((64, 64), BF16)
    S_f = p.sb((64, 64))
    ks_sb = p.sb((64, 1), BF16)
    ks_f = p.sb((64, 1))
    sc_sb = p.sb((128, 128), BF16)
    vtr = p.sb((128, 64), BF16)
    ktr = p.sb((128, 64), BF16)
    num_sb = p.sb((64, 128))
    den_sb = p.sb((1, 128))
    red = p.sb((64, 1))
    ot = p.sb((64, 128))
    ones_colb = p.sb((128, 1), BF16)
    ones64f = p.sb((1, 64))
    ps_sc = p.ps((128, 128))
    ps_num = p.ps((64, 128))
    ps_den = p.ps((1, 128))
    ps_S = p.ps((64, 64))
    ps_tr = p.ps((128, 64), BF16)
    ps_bc = p.ps((64, 128))
    ps_ks = p.ps((64, 1))

    p.load([(tril[:], trilD), (identf[:], identD)])
    p.scalar(lambda s: s.copy(out=identb[:], in_=identf[:]))
    p.vector(lambda v: (v.memset(ones_colb[:], 1.0), v.memset(ones64f[:], 1.0)))
    for h in range(2):
        r0 = h * 64
        for b in range(B):
            cc0 = b * T
            p.load([(qt[:], qR[r0:r0 + 64, cc0:cc0 + T]),
                    (kt[:], kR[r0:r0 + 64, cc0:cc0 + T]),
                    (vt[:], vR[r0:r0 + 64, cc0:cc0 + T])])
            p.scalar(lambda s: (s.copy(out=qb[:], in_=qt[:]), s.copy(out=kb[:], in_=kt[:]),
                                s.copy(out=vb[:], in_=vt[:])))
            p.vector(lambda v: (v.memset(S_sb[:], 0.0), v.memset(S_f[:], 0.0),
                                v.memset(ks_sb[:], 0.0), v.memset(ks_f[:], 0.0)))
            for c in range(T // CK):
                t0 = c * CK
                def mm1(te, t0=t0):
                    te.matmul(ps_sc[:], kb[:, t0:t0 + CK], qb[:, t0:t0 + CK],
                              start=True, stop=True)
                p.tensor(mm1)
                p.vector(lambda v: v.tensor_mul(sc_sb[:], ps_sc[:], tril[:]))
                def mm2(te, t0=t0):
                    te.transpose(ps_tr[:], vb[:, t0:t0 + CK], identb[0:64, 0:64])
                p.tensor(mm2)
                p.vector(lambda v: v.tensor_copy(vtr[:], ps_tr[:]))
                def mm3(te, t0=t0):
                    te.transpose(ps_tr[:], kb[:, t0:t0 + CK], identb[0:64, 0:64])
                p.tensor(mm3)
                p.vector(lambda v: v.tensor_copy(ktr[:], ps_tr[:]))
                def mm4(te, t0=t0):
                    te.matmul(ps_num[:], vtr[:], sc_sb[:], start=True, stop=False)
                    te.matmul(ps_num[:], S_sb[:], qb[:, t0:t0 + CK],
                              start=False, stop=True)
                    te.matmul(ps_den[:], ones_colb[:], sc_sb[:], start=True, stop=False)
                    te.matmul(ps_den[:], ks_sb[:], qb[:, t0:t0 + CK],
                              start=False, stop=True)
                    te.matmul(ps_S[:], ktr[:], vtr[:], start=True, stop=True)
                    te.matmul(ps_ks[:], ktr[:], ones_colb[:, 0:1], start=True, stop=True)
                p.tensor(mm4)
                def upd(v):
                    v.tensor_copy(num_sb[:], ps_num[:])
                    v.tensor_scalar_add(den_sb[:], ps_den[0:1, :], 0.0001)
                    v.tensor_add(S_f[:], S_f[:], ps_S[:])
                    v.tensor_add(ks_f[:], ks_f[:], ps_ks[:])
                    v.reciprocal(den_sb[:], den_sb[:])
                p.vector(upd)
                def mm5(te):
                    te.matmul(ps_bc[:], ones64f[:], den_sb[:], start=True, stop=True)
                p.tensor(mm5)
                p.vector(lambda v: v.tensor_mul(ot[:], num_sb[:], ps_bc[:]))
                p.scalar(lambda s: (s.copy(out=S_sb[:], in_=S_f[:]),
                                    s.copy(out=ks_sb[:], in_=ks_f[:])))
                p.store([(aR[r0:r0 + 64, cc0 + t0:cc0 + t0 + CK], ot[:])])
    return p.finish()


def bld_k14():
    p = Prog("k14")
    attT = p.inp("attT", (D, TS))
    x6T = p.inp("x6T", (D, TS))
    WoutT = p.inp("WoutT", (D, D))
    bout = p.inp("bout", (D, 1))
    gam = p.inp("gam", (D, 1))
    bet = p.inp("bet", (D, 1))
    U1 = p.inp("U1", (D, 256))
    V1 = p.inp("V1", (256, 4096))
    b1f = p.inp("b1f", (4096, 1))
    U2 = p.inp("U2", (4096, 256))
    V2 = p.inp("V2", (256, D))
    b2f = p.inp("b2f", (D, 1))
    x8T = p.out("x8T", (D, TS))

    nf = 8
    acc = [p.sb((128, CW)) for _ in range(nf)]
    res = [p.sb((128, CW)) for _ in range(nf)]
    sq = [p.sb((128, CW)) for _ in range(nf)]
    abf = [p.sb((128, CW), BF16) for _ in range(nf)]
    hbf = [p.sb((128, CW), BF16) for _ in range(32)]
    pbf = [p.sb((128, CW), BF16) for _ in range(2)]
    t1 = p.sb((128, CW)); t2 = p.sb((128, CW))
    wst = p.sb((128, 4096))
    wbf_o = p.sb((128, 8 * D), BF16)
    wbf_u1 = p.sb((128, 8 * 256), BF16)
    wbf_v1 = p.sb((128, 2 * 4096), BF16)
    wbf_u2 = p.sb((128, 32 * 256), BF16)
    wbf_v2 = p.sb((128, 2 * D), BF16)
    bocol = [p.sb((128, 1)) for _ in range(nf)]
    gcol = [p.sb((128, 1)) for _ in range(nf)]
    bcol = [p.sb((128, 1)) for _ in range(nf)]
    b1col = [p.sb((128, 1)) for _ in range(32)]
    b2col = [p.sb((128, 1)) for _ in range(nf)]
    ones_col = p.sb((128, 1)); ones_row = p.sb((1, 128))
    row1 = p.sb((1, CW)); row2 = p.sb((1, CW))
    ps_g = [p.ps((128, CW)) for _ in range(4)]
    ps_s1 = p.ps((1, CW)); ps_s2 = p.ps((1, CW))
    ps_b1 = p.ps((128, CW)); ps_b2 = p.ps((128, CW))

    p.vector(lambda v: (v.memset(ones_col[:], 1.0), v.memset(ones_row[:], 1.0)))
    p.load([(bocol[f][:], bout[f * 128:(f + 1) * 128, :]) for f in range(nf)]
           + [(gcol[f][:], gam[f * 128:(f + 1) * 128, :]) for f in range(nf)]
           + [(bcol[f][:], bet[f * 128:(f + 1) * 128, :]) for f in range(nf)]
           + [(b1col[m][:], b1f[m * 128:(m + 1) * 128, :]) for m in range(32)]
           + [(b2col[f][:], b2f[f * 128:(f + 1) * 128, :]) for f in range(nf)])
    emit_weight_load(p, WoutT, 0, 8, D, wst, wbf_o)
    emit_weight_load(p, U1, 0, 8, 256, wst, wbf_u1)
    emit_weight_load(p, V1, 0, 2, 4096, wst, wbf_v1)
    emit_weight_load(p, U2, 0, 32, 256, wst, wbf_u2)
    emit_weight_load(p, V2, 0, 2, D, wst, wbf_v2)

    def ln_plain(gamc=None, betc=None):
        p.scalar(lambda s: [s.activation(out=sq[f][:], in_=acc[f][:], func=AF.Square)
                            for f in range(nf)])
        def stats(te):
            for f in range(nf):
                te.matmul(ps_s1[:], ones_col[:], acc[f][:], start=(f == 0), stop=(f == nf - 1))
            for f in range(nf):
                te.matmul(ps_s2[:], ones_col[:], sq[f][:], start=(f == 0), stop=(f == nf - 1))
        p.tensor(stats)
        def rowops(v):
            v.tensor_scalar_mul(row1[:], ps_s1[0:1, :], 1.0 / 1024.0)
            v.tensor_scalar_mul(row2[:], ps_s2[0:1, :], 1.0 / 1024.0)
            v.tensor_mul(sq[0][0:1, :], row1[:], row1[:])
            v.tensor_sub(row2[:], row2[:], sq[0][0:1, :])
            v.tensor_scalar_add(row2[:], row2[:], 1e-5)
        p.vector(rowops)
        p.scalar(lambda s: (s.activation(out=row2[:], in_=row2[:], func=AF.Ln),
                            s.activation(out=row2[:], in_=row2[:], func=AF.Exp, scale=-0.5)))
        def bc(te):
            te.matmul(ps_b1[:], ones_row[:], row1[:], start=True, stop=True)
            te.matmul(ps_b2[:], ones_row[:], row2[:], start=True, stop=True)
        p.tensor(bc)
        def norm(v):
            for f in range(nf):
                v.tensor_sub(acc[f][:], acc[f][:], ps_b1[:])
                v.tensor_mul(acc[f][:], acc[f][:], ps_b2[:])
                if gamc is not None:
                    v.tensor_scalar(acc[f][:], acc[f][:], gamc[f][:], betc[f][:], ALU.mult, ALU.add)
        p.vector(norm)

    for ci in range(NCH):
        c0 = ci * CW
        p.load([(sq[f][:], attT[f * 128:(f + 1) * 128, c0:c0 + CW]) for f in range(nf)])
        p.scalar(lambda s: [s.copy(out=abf[f][:], in_=sq[f][:]) for f in range(nf)])
        for mg in range(2):
            mlist = [mg * 4 + i for i in range(4)]
            emit_matmul(p, wbf_o, D, 8, abf, ps_g, mlist, CW)
            def evac(v, mlist=mlist):
                for mi, m in enumerate(mlist):
                    v.tensor_scalar_add(acc[m][:], ps_g[mi][:], bocol[m][:])
            p.vector(evac)
        ln_plain(gamc=gcol, betc=bcol)
        p.load([(res[f][:], x6T[f * 128:(f + 1) * 128, c0:c0 + CW]) for f in range(nf)])
        p.vector(lambda v: [v.tensor_add(acc[f][:], acc[f][:], res[f][:]) for f in range(nf)])
        ln_plain()
        p.vector(lambda v: [v.tensor_copy(res[f][:], acc[f][:]) for f in range(nf)])
        p.scalar(lambda s: [s.copy(out=abf[f][:], in_=acc[f][:]) for f in range(nf)])
        mlist = [0, 1]
        emit_matmul(p, wbf_u1, 256, 8, abf, ps_g, mlist, CW)
        p.scalar(lambda s: [s.copy(out=pbf[m][:], in_=ps_g[m][:]) for m in range(2)])
        for mg in range(8):
            mlist = [mg * 4 + i for i in range(4)]
            emit_matmul(p, wbf_v1, 4096, 2, pbf, ps_g, mlist, CW)
            for mi, m in enumerate(mlist):
                def gel(v, mi=mi, m=m):
                    v.tensor_scalar_add(t1[:], ps_g[mi][:], b1col[m][:])
                    v.tensor_mul(t2[:], t1[:], t1[:])
                    v.tensor_mul(t2[:], t2[:], t1[:])
                    v.scalar_tensor_tensor(t2[:], t2[:], GELU_C1, t1[:], ALU.mult, ALU.add)
                p.vector(gel)
                p.scalar(lambda s: s.activation(out=t2[:], in_=t2[:], func=AF.Tanh, scale=GELU_C0))
                def gel2(v):
                    v.tensor_scalar(t2[:], t2[:], 0.5, 0.5, ALU.mult, ALU.add)
                    v.tensor_mul(t2[:], t2[:], t1[:])
                p.vector(gel2)
                p.scalar(lambda s, m=m: s.copy(out=hbf[m][:], in_=t2[:]))
        mlist = [0, 1]
        emit_matmul(p, wbf_u2, 256, 32, hbf, ps_g, mlist, CW)
        p.scalar(lambda s: [s.copy(out=pbf[m][:], in_=ps_g[m][:]) for m in range(2)])
        for mg in range(2):
            mlist = [mg * 4 + i for i in range(4)]
            emit_matmul(p, wbf_v2, D, 2, pbf, ps_g, mlist, CW)
            def evac(v, mlist=mlist):
                for mi, m in enumerate(mlist):
                    v.tensor_scalar_add(t1[:], ps_g[mi][:], b2col[m][:])
                    v.tensor_add(acc[m][:], res[m][:], t1[:])
            p.vector(evac)
        ln_plain()
        for f in range(nf):
            p.store([(x8T[f * 128:(f + 1) * 128, c0:c0 + CW], acc[f][:])])
    return p.finish()



def _get(name, builder):
    if name not in _PROG_CACHE:
        _PROG_CACHE[name] = builder()
    return _PROG_CACHE[name]


def _run(nc, in_maps, trace=False):
    res = run_bass_kernel_spmd(nc, in_maps, core_ids=list(range(NCORES)), trace=trace)
    return res.results


def _tok_slices(full, halo=0):
    out = []
    F = full.shape[0]
    for c in range(NCORES):
        t0 = c * TS
        sl = np.zeros((F, TS + halo), dtype=full.dtype)
        sl[:, halo:] = full[:, t0:t0 + TS]
        if halo and (c % 2 == 1):
            sl[:, :halo] = full[:, t0 - halo:t0]
        out.append(np.ascontiguousarray(sl if halo else full[:, t0:t0 + TS]))
    return out


def _row_slices(full, rows_per_core):
    return [np.ascontiguousarray(full[c * rows_per_core:(c + 1) * rows_per_core, :])
            for c in range(NCORES)]


def _cat_tok(results, key):
    return np.concatenate([r[key] for r in results], axis=1)


def _cat_row(results, key):
    return np.concatenate([r[key] for r in results], axis=0)


def kernel(**inp):
    f32 = np.float32
    cm = np.ascontiguousarray
    ident = np.eye(128, dtype=f32)
    tril = np.tril(np.ones((128, 128), dtype=f32))
    triu = np.triu(np.ones((128, 128), dtype=f32))

    x = inp["L0_attn_x"].astype(f32)
    xT = cm(x.reshape(N, D).T)

    col = lambda a: cm(np.asarray(a, f32).reshape(-1, 1))

    nc = _get("k1", bld_k1)
    maps = []
    xs = _tok_slices(xT)
    for c in range(NCORES):
        maps.append({"xT": xs[c], "W_in": cm(np.asarray(inp["L0_attn_W_in"], f32)),
                     "b_in": col(inp["L0_attn_b_in"]),
                     "gamma": col(inp["L0_attn_gamma"]), "beta": col(inp["L0_attn_beta"])})
    res = _run(nc, maps)
    projT = _cat_tok(res, "projT")
    DBG['projT'] = projT

    nc = _get("k2", bld_k2)
    pslices = _tok_slices(projT, halo=2)
    maps = [{"projS": pslices[c], "conv_w": cm(np.asarray(inp["L0_attn_conv_w"], f32).reshape(2048, 3).T.reshape(-1, 1)),
             "conv_b": col(inp["L0_attn_conv_b"]),
             "Wa": cm(np.asarray(inp["L0_attn_Wa"], f32)), "Wb": cm(np.asarray(inp["L0_attn_Wb"], f32)),
             "Wc": cm(np.asarray(inp["L0_attn_Wc"], f32)),
             "ba": col(inp["L0_attn_ba"]), "bb": col(inp["L0_attn_bb"]), "bc": col(inp["L0_attn_bc"])}
            for c in range(NCORES)]
    res = _run(nc, maps)
    aT = _cat_tok(res, "aT"); vT = _cat_tok(res, "vT")
    DBG['aT'] = aT
    cT = _cat_tok(res, "cT"); uT = _cat_tok(res, "uT")
    DBG['cT'] = cT

    nc = _get("k3", bld_k3)
    maps = [{"aR": s_a, "vR": s_v, "cR": s_c, "uR": s_u}
            for s_a, s_v, s_c, s_u in zip(_row_slices(aT, 256), _row_slices(vT, 256),
                                          _row_slices(cT, 256), _row_slices(uT, 256))]
    res = _run(nc, maps)
    yT = _cat_row(res, "yR")
    DBG['yT'] = yT

    nc = _get("k4", bld_k4)
    ysl = _tok_slices(yT); xsl = _tok_slices(xT)
    maps = [{"yT": ysl[c], "xT": xsl[c], "Wo": cm(np.asarray(inp["L0_attn_Wo"], f32)),
             "bo": col(inp["L0_attn_bo"]), "w1": cm(np.asarray(inp["L0_ffn_w1"], f32)),
             "w2": cm(np.asarray(inp["L0_ffn_w2"], f32)),
             "ln_g": col(inp["L1_attn_ln_g"]), "ln_b": col(inp["L1_attn_ln_b"]),
             "W_in1": cm(np.asarray(inp["L1_attn_W_in"], f32))}
            for c in range(NCORES)]
    res = _run(nc, maps)
    x2T = _cat_tok(res, "x2T"); xzT = _cat_tok(res, "xzT")
    DBG['x2T'] = x2T

    nc = _get("k5", bld_k5)
    xz_x = xzT[0:D, :]
    zT = xzT[D:2 * D, :]
    xzsl = _tok_slices(xz_x, halo=3)
    maps = [{"xzS": xzsl[c], "conv_w": cm(np.asarray(inp["L1_attn_conv_w"], f32).reshape(D, 4).T.reshape(-1, 1)),
             "conv_b": col(inp["L1_attn_conv_b"]), "dt_w": cm(np.asarray(inp["L1_attn_dt_w"], f32)),
             "dt_b": col(inp["L1_attn_dt_b"]), "B_w": cm(np.asarray(inp["L1_attn_B_w"], f32)),
             "C_w": cm(np.asarray(inp["L1_attn_C_w"], f32)), "logA": col(inp["L1_attn_log_A"])}
            for c in range(NCORES)]
    res = _run(nc, maps)
    xsT = _cat_tok(res, "xsT"); decT = _cat_tok(res, "decT"); dtxT = _cat_tok(res, "dtxT")
    DBG['xsT'] = xsT
    BmT = _cat_tok(res, "BmT"); CmT = _cat_tok(res, "CmT")
    DBG['BmT'] = BmT

    nc = _get("k6", bld_k6)
    maps = [{"decR": s_d, "dtxR": s_x, "BmF": cm(BmT), "CmF": cm(CmT)}
            for s_d, s_x in zip(_row_slices(decT, 128), _row_slices(dtxT, 128))]
    res = _run(nc, maps)
    ysT = _cat_row(res, "yR")
    DBG['ysT'] = ysT

    nc = _get("k7", bld_k7)
    maps = [{"ysT": a, "xsT": b_, "zT": c_, "Dp": col(inp["L1_attn_D_param"]),
             "W_out": cm(np.asarray(inp["L1_attn_W_out"], f32)), "x2T": d_}
            for a, b_, c_, d_ in zip(_tok_slices(ysT), _tok_slices(xsT),
                                     _tok_slices(zT), _tok_slices(x2T))]
    res = _run(nc, maps)
    x3T = _cat_tok(res, "x3T")
    DBG['x3T'] = x3T

    nc = _get("k8", bld_k8)
    x3sl = _tok_slices(x3T, halo=1)
    WrT = cm(np.asarray(inp["L1_ffn_W_r"], f32).T)
    WkT = cm(np.asarray(inp["L1_ffn_W_k"], f32).T)
    WvT = cm(np.asarray(inp["L1_ffn_W_v"], f32).T)
    Wz = np.asarray(inp["L2_attn_Wz"], f32); Wh = np.asarray(inp["L2_attn_Wh"], f32)
    bz = np.asarray(inp["L2_attn_bz"], f32); bh = np.asarray(inp["L2_attn_bh"], f32)
    maps = [{"x3S": x3sl[c], "W_rT": WrT, "W_kT": WkT, "W_vT": WvT,
             "mu_r": col(inp["L1_ffn_mu_r"]), "mu_k": col(inp["L1_ffn_mu_k"]),
             "WzT0": cm(Wz[0].T), "bz0": col(bz[0]), "WhT0": cm(Wh[0].T), "bh0": col(bh[0])}
            for c in range(NCORES)]
    res = _run(nc, maps)
    x4T = _cat_tok(res, "x4T"); coef1T = _cat_tok(res, "coefT"); win1T = _cat_tok(res, "winT")
    DBG['x4T'] = x4T

    nc = _get("k9", bld_k9)
    maps = [{"aR": a, "vR": v_} for a, v_ in zip(_row_slices(coef1T, 128), _row_slices(win1T, 128))]
    res = _run(nc, maps)
    h1T = _cat_row(res, "hR")
    DBG['h1T'] = h1T

    nc = _get("k10", bld_k10)
    maps = [{"h1T": s, "WzT1": cm(Wz[1].T), "bz1": col(bz[1]),
             "WhT1": cm(Wh[1].T), "bh1": col(bh[1])} for s in _tok_slices(h1T)]
    res = _run(nc, maps)
    coef2T = _cat_tok(res, "coefT"); win2T = _cat_tok(res, "winT")
    DBG['coef2T'] = coef2T

    nc = _get("k9", bld_k9)
    maps = [{"aR": a, "vR": v_} for a, v_ in zip(_row_slices(coef2T, 128), _row_slices(win2T, 128))]
    res = _run(nc, maps)
    h2T = _cat_row(res, "hR")
    DBG['h2T'] = h2T

    Wg = np.asarray(inp["L2_ffn_Wg"], f32); bg = np.asarray(inp["L2_ffn_bg"], f32)
    W1 = np.asarray(inp["L2_ffn_W1"], f32); b1 = np.asarray(inp["L2_ffn_b1"], f32)
    W2 = np.asarray(inp["L2_ffn_W2"], f32); b2 = np.asarray(inp["L2_ffn_b2"], f32)
    curT = h2T
    x4sl = _tok_slices(x4T)
    for r in range(4):
        nc = _get(f"k12r{r}", lambda r=r: bld_k12b(r))
        cursl = _tok_slices(curT)
        maps = []
        for c in range(NCORES):
            m = {"curT": cursl[c], "Wg": cm(Wg[r]), "bg": cm(bg[r:r + 1]),
                 "W1": cm(W1.reshape(4 * D, 4096)), "b1": cm(b1.reshape(-1, 1)),
                 "W2": cm(W2.reshape(4 * 4096, D)), "b2": cm(b2.reshape(-1, 1)),
                 "identD": ident}
            if r == 0:
                m["x4T"] = x4sl[c]
            maps.append(m)
        res = _run(nc, maps)
        curT = _cat_tok(res, "outT")
        DBG[f'cur{r}'] = curT
        if r == 0:
            x5T = _cat_tok(res, "x5T")
            DBG['x5T'] = x5T

    nc = _get("k12q", bld_k12q)
    maps = [{"curT": a, "x5Ti": b_, "WqkvT": cm(np.asarray(inp["L3_attn_Wqkv"], f32).T),
             "bqkv": col(inp["L3_attn_bqkv"])}
            for a, b_ in zip(_tok_slices(curT), _tok_slices(x5T))]
    res = _run(nc, maps)
    qpT = _cat_tok(res, "qpT"); kpT = _cat_tok(res, "kpT"); vT3 = _cat_tok(res, "vT")
    x6T = _cat_tok(res, "x6To")
    DBG['qpT'] = qpT
    DBG['x6T'] = x6T

    nc = _get("k13", bld_k13)
    maps = [{"qR": a, "kR": b_, "vR": c_, "trilD": triu, "identD": ident}
            for a, b_, c_ in zip(_row_slices(qpT, 128), _row_slices(kpT, 128),
                                 _row_slices(vT3, 128))]
    res = _run(nc, maps)
    attT = _cat_row(res, "aR")
    DBG['attT'] = attT

    nc = _get("k14", bld_k14)
    maps = [{"attT": a, "x6T": b_, "WoutT": cm(np.asarray(inp["L3_attn_Wout"], f32).T),
             "bout": col(inp["L3_attn_bout"]), "gam": col(inp["L3_attn_gamma"]),
             "bet": col(inp["L3_attn_beta"]), "U1": cm(np.asarray(inp["L3_ffn_U1"], f32)),
             "V1": cm(np.asarray(inp["L3_ffn_V1"], f32)), "b1f": col(inp["L3_ffn_b1"]),
             "U2": cm(np.asarray(inp["L3_ffn_U2"], f32)), "V2": cm(np.asarray(inp["L3_ffn_V2"], f32)),
             "b2f": col(inp["L3_ffn_b2"])}
            for a, b_ in zip(_tok_slices(attT), _tok_slices(x6T))]
    res = _run(nc, maps)
    x8T = _cat_tok(res, "x8T")
    DBG['x8T'] = x8T

    out = x8T.reshape(D, B, T).transpose(1, 2, 0)
    return np.ascontiguousarray(out).astype(np.float32)
